# version 05
# baseline (speedup 1.0000x reference)
import contextlib
import numpy as np

import concourse.bass as bass
import concourse.mybir as mybir
from concourse.bass_utils import run_bass_kernel_spmd

F32 = mybir.dt.float32
BF16 = mybir.dt.bfloat16
FP8 = mybir.dt.float8e4
I32 = mybir.dt.int32
AF = mybir.ActivationFunctionType
ALU = mybir.AluOpType
AXX = mybir.AxisListType.X

D = 2048
NH = 16
NT = 1024
NG = NT // 128
NC = 16
IDIM = 8192
IB = 256
NIB = IDIM // IB
S128 = float(1.0 / np.sqrt(128.0))
NEG = -1e30
B23 = float(1 << 23)


class Prog:
    ENGS = ("sync", "vector", "scalar", "tensor", "gpsimd")

    def __init__(self):
        self.ops = []
        self.cnt = {}

    def op(self, eng, fn, waits=(), inc=None, incd=1):
        i = None
        if inc is not None:
            self.cnt[inc] = self.cnt.get(inc, 0) + incd
            i = (inc, self.cnt[inc])
        self.ops.append((eng, fn, tuple(waits), i, incd))

    def c(self, sem):
        return self.cnt.get(sem, 0)

    def barrier(self):
        snap = [(k, self.cnt.get(k, 0)) for k in ("dS", "dG", "sv", "ss", "st", "sg")]
        marks = {}
        for eng, sem in (("sync", "qS"), ("gpsimd", "sg"), ("vector", "sv"),
                         ("scalar", "ss"), ("tensor", "st")):
            self.op(eng, (lambda e: e.nop()), waits=[p for p in snap], inc=sem)
            marks[sem] = self.c(sem)
        snap2 = [(k, v) for k, v in marks.items()]
        for eng in ("sync", "gpsimd", "vector", "scalar", "tensor"):
            self.op(eng, (lambda e: e.nop()), waits=snap2)

    def emit(self, block, sems):
        for name in self.ENGS:
            ops = [o for o in self.ops if o[0] == name]
            if not ops:
                continue

            def make(ops_):
                def body(eng):
                    last_w = {}
                    for _, fn, waits, inc, incd in ops_:
                        for s, v in waits:
                            if v > 0 and last_w.get(s, -1) < v:
                                eng.wait_ge(sems[s], v)
                                last_w[s] = v
                        r = fn(eng)
                        if inc is not None:
                            r.then_inc(sems[inc[0]], incd)
                return body

            getattr(block, name)(make(ops))


def build_nc(dbg=()):
    nc = bass.Bass("TRN2", target_bir_lowering=False, detect_race_conditions=False)
    es = contextlib.ExitStack()

    ap = {}
    for name, shape in [("x_own", [NT, D]), ("x_pre", [NT, D]),
                        ("w_qkv", [3 * D, D]), ("b_qkv", [3 * D]),
                        ("w_out", [D, D]), ("b_out", [D]),
                        ("w_gate", [IDIM, D]), ("w_up", [IDIM, D]),
                        ("w_down", [D, IDIM]), ("c_row", [1, D]),
                        ("wmod", [D, 6 * D]), ("kvm", [1, 2])]:
        ap[name] = nc.dram_tensor(name, shape, F32, kind="ExternalInput").ap()
    y = nc.dram_tensor("y", [NT, D], F32, kind="ExternalOutput").ap()
    qwdT_d = nc.dram_tensor(
        "qwdT_d", [64, 128, D], BF16,
        kind=("ExternalOutput" if "dbg_qwd" in dbg else "Internal")).ap()
    qiT_d = nc.dram_tensor(
        "qiT_d", [64, 128, NT], BF16,
        kind=("ExternalOutput" if "dbg_qiT" in dbg else "Internal")).ap()

    dbg_out = {}
    for name, shape in [("dbg_h1", [2 * NT, D]), ("dbg_x1", [NT, D]),
                        ("dbg_qh2", [NT, D]), ("dbg_mod", [1, 6 * D]),
                        ("dbg_O", [NT, D])]:
        if name in dbg:
            dbg_out[name] = nc.dram_tensor(name, shape, F32,
                                           kind="ExternalOutput").ap()
    for name, shape in [("dbg_K", [128, 2 * NT]), ("dbg_Q", [128, NT]),
                        ("dbg_V", [128, 2 * NT]), ("dbg_AROW", [128, 2 * 2048]),
                        ("dbg_AT", [128, NC * 256])]:
        if name in dbg:
            dbg_out[name] = nc.dram_tensor(name, shape, BF16,
                                           kind="ExternalOutput").ap()
    if "dbg_DEN" in dbg:
        dbg_out["dbg_DEN"] = nc.dram_tensor("dbg_DEN", [128, 8], F32,
                                            kind="ExternalOutput").ap()
    if "dbg_LN" in dbg:
        dbg_out["dbg_LN"] = nc.dram_tensor("dbg_LN", [128, 64], F32,
                                           kind="ExternalOutput").ap()
    if "dbg_H1F" in dbg:
        dbg_out["dbg_H1F"] = nc.dram_tensor("dbg_H1F", [128, NC * 2 * NT], BF16,
                                            kind="ExternalOutput").ap()

    sems = {}
    for snm in ("dS", "dG", "sv", "ss", "st", "sg", "qS"):
        sems[snm] = es.enter_context(nc.semaphore(snm))
    P = Prog()

    def mksb(stack):
        def f(name, shape, dt_):
            return stack.enter_context(nc.sbuf_tensor(name, shape, dt_))
        return f

    sb = mksb(es)

    identb = sb("identb", [128, 128], BF16)
    identf = sb("identf", [128, 128], F32)
    causal = sb("causal", [128, 128], F32)
    kvm_sb = sb("kvm_sb", [128, 2], F32)
    consts = sb("consts", [128, 4], F32)
    onesb = sb("onesb", [1, 128], BF16)
    modv = sb("modv", [128, 6 * NC], F32)
    CAt = sb("CAt", [128, D], F32)
    SAt = sb("SAt", [128, D], F32)
    bqkv_sb = sb("bqkv_sb", [128, 3 * NC], F32)
    bout_ga = sb("bout_ga", [128, NC], F32)
    stats = sb("stats", [128, 16], F32)
    recipd = sb("recipd", [128, NG], F32)
    lnbf = sb("lnbf", [128, D], BF16)
    bounce = sb("bounce", [128, 4 * 128], BF16)
    qf8 = sb("qf8", [128, 2048], FP8)
    mod_sbcb = sb("mod_sbcb", [1, 2 * D], BF16)
    xs = sb("xs", [128, NG, D], F32)
    psA = es.enter_context(nc.psum_tensor("psA", [128, 2048], F32))
    psB = es.enter_context(nc.psum_tensor("psB", [128, 512], F32))
    psC = es.enter_context(nc.psum_tensor("psC", [128, 1024], BF16))
    psD = es.enter_context(nc.psum_tensor("psD", [128, 1024], BF16))
    psE = es.enter_context(nc.psum_tensor("psE", [128, 512], F32))

    bounce_last = [0, 0, 0, 0]

    def psum_to_dram_bf16(pt, dram_slice):
        slot = psum_to_dram_bf16.n % 4
        psum_to_dram_bf16.n += 1
        P.op("scalar", lambda sc, pt=pt, slot=slot: sc.activation(
            out=bounce[:, slot * 128:(slot + 1) * 128], in_=pt[:, 0:128],
            func=AF.Copy),
            waits=[("st", P.c("st")), ("dG", bounce_last[slot])])
        P.op("scalar", lambda sc: sc.drain(), inc="ss")
        P.op("gpsimd", lambda g, slot=slot, dram_slice=dram_slice: g.dma_start(
            out=dram_slice, in_=bounce[:, slot * 128:(slot + 1) * 128]),
            waits=[("ss", P.c("ss"))], inc="dG", incd=16)
        bounce_last[slot] = P.c("dG")

    psum_to_dram_bf16.n = 0

    P.op("gpsimd", lambda g: g.memset(identb[:], 0.0), inc="sg")
    P.op("gpsimd", lambda g: g.affine_select(
        out=identb[:], in_=identb[:], compare_op=ALU.not_equal, fill=1.0,
        base=0, pattern=[[-1, 128]], channel_multiplier=1),
        waits=[("sg", P.c("sg"))], inc="sg")
    P.op("gpsimd", lambda g: g.memset(identf[:], 0.0),
        waits=[("sg", P.c("sg"))], inc="sg")
    P.op("gpsimd", lambda g: g.affine_select(
        out=identf[:], in_=identf[:], compare_op=ALU.not_equal, fill=1.0,
        base=0, pattern=[[-1, 128]], channel_multiplier=1),
        waits=[("sg", P.c("sg"))], inc="sg")
    P.op("gpsimd", lambda g: g.memset(causal[:], 0.0),
        waits=[("sg", P.c("sg"))], inc="sg")
    P.op("gpsimd", lambda g: g.affine_select(
        out=causal[:], in_=causal[:], compare_op=ALU.is_ge, fill=NEG,
        base=0, pattern=[[-1, 128]], channel_multiplier=1),
        waits=[("sg", P.c("sg"))], inc="sg")
    P.op("gpsimd", lambda g: g.memset(onesb[:], 1.0),
        waits=[("sg", P.c("sg"))], inc="sg")
    P.op("vector", lambda v: v.memset(consts[:, 0:1], 1e-5), inc="sv")
    P.op("sync", lambda s: s.dma_start(
        out=kvm_sb[:], in_=ap["kvm"][0:1, :].broadcast_to((128, 2))),
        inc="dS", incd=16)

    def bcast_row(dst, src_row_bf, add1, wait_extra=()):
        for oc in range(D // 512):
            bank = oc % 4
            P.op("tensor", lambda t, oc=oc, bank=bank: t.matmul(
                psA[:, bank * 512:(bank + 1) * 512], onesb[0:1, :],
                src_row_bf[0:1, oc * 512:(oc + 1) * 512],
                start=True, stop=True),
                waits=list(wait_extra) + [("ss", P.c("ss")), ("sg", P.c("sg")),
                                          ("sv", P.c("sv"))],
                inc="st")
        P.op("vector", lambda v: v.tensor_scalar(
            out=dst, in0=psA[:, 0:D], scalar1=(1.0 if add1 else 0.0),
            scalar2=None, op0=ALU.add, op1=ALU.bypass),
            waits=[("st", P.c("st"))], inc="sv")

    es1 = contextlib.ExitStack()
    sb1 = mksb(es1)
    mod_row = sb1("mod_row", [1, 6 * D], F32)
    mod_bf = sb1("mod_bf", [1, 6 * D], BF16)
    cs = sb1("cs", [1, D], F32)
    csb = sb1("csb", [1, D], BF16)
    cT = sb1("cT", [128, NC], BF16)
    wm_st = sb1("wm_st", [128, 2 * 512], F32)
    wm_bf = sb1("wm_bf", [128, 2 * 512], BF16)
    gtmp = sb1("gtmp", [128, 4 * NC], F32)

    P.op("sync", lambda s: s.dma_start(out=cs[:], in_=ap["c_row"][:, :]),
         inc="dS", incd=16)
    P.op("scalar", lambda sc: sc.activation(out=csb[:], in_=cs[:], func=AF.Silu),
         waits=[("dS", P.c("dS"))], inc="ss")
    for j in range(NC):
        P.op("tensor", lambda t, j=j: t.transpose(
            psC[:, 0:1], csb[0:1, j * 128:(j + 1) * 128], identb[0:1, 0:1]),
            waits=[("ss", P.c("ss")), ("sg", P.c("sg")), ("sv", P.c("sv"))],
            inc="st")
        P.op("vector", lambda v, j=j: v.tensor_copy(cT[:, j:j + 1], psC[:, 0:1]),
             waits=[("st", P.c("st"))], inc="sv")
    NOC = 6 * D // 512
    for oc in range(NOC):
        for j in range(NC):
            buf = (oc * NC + j) % 2
            P.op("sync", lambda s, oc=oc, j=j, buf=buf: s.dma_start(
                out=wm_st[:, buf * 512:(buf + 1) * 512],
                in_=ap["wmod"][j * 128:(j + 1) * 128, oc * 512:(oc + 1) * 512]),
                waits=[("ss", P.c("ss"))], inc="dS", incd=16)
            P.op("scalar", lambda sc, buf=buf: sc.activation(
                out=wm_bf[:, buf * 512:(buf + 1) * 512],
                in_=wm_st[:, buf * 512:(buf + 1) * 512], func=AF.Copy),
                waits=[("dS", P.c("dS")), ("st", P.c("st"))], inc="ss")
            P.op("tensor", lambda t, j=j, buf=buf: t.matmul(
                psE[0:1, 0:512], cT[:, j:j + 1],
                wm_bf[:, buf * 512:(buf + 1) * 512],
                start=(j == 0), stop=(j == NC - 1)),
                waits=[("ss", P.c("ss")), ("sv", P.c("sv"))], inc="st")
        P.op("vector", lambda v, oc=oc: v.tensor_copy(
            mod_row[0:1, oc * 512:(oc + 1) * 512], psE[0:1, 0:512]),
            waits=[("st", P.c("st"))], inc="sv")
    P.op("scalar", lambda sc: sc.activation(out=mod_bf[:], in_=mod_row[:],
                                            func=AF.Copy),
         waits=[("sv", P.c("sv"))], inc="ss")
    P.op("scalar", lambda sc: sc.activation(
        out=mod_sbcb[:], in_=mod_row[0:1, 3 * D:5 * D], func=AF.Copy), inc="ss")

    for k in range(6):
        for j in range(NC):
            seg = k * D + j * 128
            P.op("tensor", lambda t, seg=seg: t.transpose(
                psE[:, 0:1], mod_row[0:1, seg:seg + 128], identf[0:1, 0:1]),
                waits=[("sv", P.c("sv")), ("ss", P.c("ss"))], inc="st")
            P.op("vector", lambda v, k=k, j=j: v.tensor_copy(
                modv[:, k * NC + j:k * NC + j + 1], psE[:, 0:1]),
                waits=[("st", P.c("st"))], inc="sv")
    for g in (modv[:, 2 * NC:3 * NC], modv[:, 5 * NC:6 * NC]):
        a, b_ = gtmp[:, 0:NC], gtmp[:, NC:2 * NC]
        m, mb = gtmp[:, 2 * NC:3 * NC], gtmp[:, 3 * NC:4 * NC]
        P.op("vector", lambda v, g=g, a=a: v.tensor_scalar(
            out=a, in0=g, scalar1=0.25, scalar2=None, op0=ALU.max, op1=ALU.bypass))
        P.op("vector", lambda v, g=g, b_=b_: v.tensor_scalar(
            out=b_, in0=g, scalar1=-0.25, scalar2=None, op0=ALU.min, op1=ALU.bypass))
        P.op("vector", lambda v, g=g, m=m: v.tensor_scalar(
            out=m, in0=g, scalar1=0.0, scalar2=None, op0=ALU.is_ge, op1=ALU.bypass))
        P.op("vector", lambda v: v.drain())
        P.op("vector", lambda v, m=m, mb=mb: v.tensor_scalar(
            out=mb, in0=m, scalar1=-1.0, scalar2=1.0, op0=ALU.mult, op1=ALU.add))
        P.op("vector", lambda v, m=m, a=a: v.tensor_tensor(
            out=a, in0=m, in1=a, op=ALU.mult))
        P.op("vector", lambda v: v.drain())
        P.op("vector", lambda v, mb=mb, b_=b_: v.tensor_tensor(
            out=b_, in0=mb, in1=b_, op=ALU.mult))
        P.op("vector", lambda v: v.drain())
        P.op("vector", lambda v, g=g, a=a, b_=b_: v.tensor_tensor(
            out=g, in0=a, in1=b_, op=ALU.add))
        P.op("vector", lambda v: v.drain(), inc="sv")
    if "dbg_mod" in dbg_out:
        P.op("sync", lambda s: s.dma_start(out=dbg_out["dbg_mod"][:, :],
                                           in_=mod_row[:]),
             waits=[("sv", P.c("sv"))], inc="dS", incd=16)
        P.op("sync", lambda s: s.nop(), waits=[("dS", P.c("dS"))])

    bcast_row(CAt[:], mod_bf[0:1, 1 * D:2 * D].unsqueeze(0).squeeze(0)
              if False else mod_bf[0:1, 1 * D:2 * D], True,
              wait_extra=[("ss", P.c("ss"))])
    bcast_row(SAt[:], mod_bf[0:1, 0 * D:1 * D], False)

    P.barrier()
    es1.close()
    es1b = contextlib.ExitStack()
    sb1b = mksb(es1b)
    bq_st = sb1b("bq_st", [1, 3 * D], F32)
    bo_st = sb1b("bo_st", [1, D], F32)
    P.op("sync", lambda s: s.dma_start(out=bq_st[0:1, :],
                                       in_=ap["b_qkv"][:].unsqueeze(0)),
         inc="dS", incd=16)
    P.op("sync", lambda s: s.dma_start(out=bo_st[0:1, :],
                                       in_=ap["b_out"][:].unsqueeze(0)),
         inc="dS", incd=16)
    _bw = P.c("dS")
    for j in range(3 * NC):
        P.op("tensor", lambda t, j=j: t.transpose(
            psE[:, 0:1], bq_st[0:1, j * 128:(j + 1) * 128], identf[0:1, 0:1]),
            waits=[("dS", _bw), ("sv", P.c("sv"))], inc="st")
        sc_ = S128 if j < NC else 1.0
        P.op("vector", lambda v, j=j, sc_=sc_: v.tensor_scalar(
            out=bqkv_sb[:, j:j + 1], in0=psE[:, 0:1], scalar1=sc_,
            scalar2=None, op0=ALU.mult, op1=ALU.bypass),
            waits=[("st", P.c("st"))], inc="sv")
    for j in range(NC):
        P.op("tensor", lambda t, j=j: t.transpose(
            psE[:, 0:1], bo_st[0:1, j * 128:(j + 1) * 128], identf[0:1, 0:1]),
            waits=[("dS", _bw), ("sv", P.c("sv"))], inc="st")
        P.op("vector", lambda v, j=j: v.tensor_tensor(
            out=bout_ga[:, j:j + 1], in0=psE[:, 0:1],
            in1=modv[:, 2 * NC + j:2 * NC + j + 1], op=ALU.mult),
            waits=[("st", P.c("st"))], inc="sv")
    P.barrier()
    es1b.close()

    def emit_ln_modulate(xtile, xhat, out_tile, ca_t, sa_t, waits0=()):
        a1, a2 = stats[:, 0:1], stats[:, 1:2]
        m, m2 = stats[:, 2:3], stats[:, 3:4]
        va, rstd, nmr = stats[:, 4:5], stats[:, 5:6], stats[:, 6:7]
        lnv = stats[:, 7:8]
        P.op("scalar", lambda sc: sc.activation(
            out=lnbf[:], in_=xtile, func=AF.Copy, accum_out=a1),
            waits=list(waits0) + [("sv", P.c("sv"))], inc="ss")
        P.op("scalar", lambda sc: sc.activation(
            out=lnbf[:], in_=xtile, func=AF.Square, accum_out=a2))
        P.op("scalar", lambda sc: sc.drain(), inc="ss")
        sw = P.c("ss")
        P.op("vector", lambda v: v.tensor_scalar(
            out=m, in0=a1, scalar1=1.0 / D, scalar2=None, op0=ALU.mult,
            op1=ALU.bypass), waits=[("ss", sw)])
        P.op("vector", lambda v: v.drain())
        P.op("vector", lambda v: v.tensor_tensor(out=m2, in0=m, in1=m, op=ALU.mult))
        P.op("vector", lambda v: v.drain())
        P.op("vector", lambda v: v.scalar_tensor_tensor(
            out=va, in0=a2, scalar=1.0 / D, in1=m2, op0=ALU.mult, op1=ALU.subtract))
        P.op("vector", lambda v: v.drain(), inc="sv")
        P.op("scalar", lambda sc: sc.activation(
            out=lnv, in_=va, func=AF.Ln, bias=consts[:, 0:1], scale=1.0),
            waits=[("sv", P.c("sv"))])
        P.op("scalar", lambda sc: sc.drain())
        P.op("scalar", lambda sc: sc.activation(
            out=rstd, in_=lnv, func=AF.Exp, scale=-0.5))
        P.op("scalar", lambda sc: sc.drain(), inc="ss")
        P.op("vector", lambda v: v.scalar_tensor_tensor(
            out=nmr, in0=m, scalar=-1.0, in1=rstd, op0=ALU.mult, op1=ALU.mult),
            waits=[("ss", P.c("ss"))])
        P.op("vector", lambda v: v.drain(), inc="sv")
        P.op("scalar", lambda sc: sc.activation(
            out=xhat, in_=xtile, func=AF.Identity, bias=nmr, scale=rstd),
            waits=[("sv", P.c("sv"))], inc="ss")
        P.op("vector", lambda v: v.tensor_tensor(out=xhat, in0=xhat, in1=ca_t,
                                                 op=ALU.mult),
             waits=[("ss", P.c("ss"))])
        P.op("vector", lambda v: v.tensor_tensor(out=out_tile, in0=xhat, in1=sa_t,
                                                 op=ALU.add),
             waits=[("st", P.c("st"))], inc="sv")

    def dump_fm(dst_dram, src_fm, ngroups, stack_sb):
        dbt = stack_sb(f"dbt{dump_fm.n}", [128, D], F32)
        dump_fm.n += 1
        for g in range(ngroups):
            dw = P.c("dS")
            for j in range(NC):
                pt = psC if j % 2 == 0 else psD
                P.op("tensor", lambda t, g=g, j=j, pt=pt: t.transpose(
                    pt[:, 0:128], src_fm[:, j, g * 128:(g + 1) * 128], identb[:]),
                    waits=[("ss", P.c("ss")), ("sv", P.c("sv"))], inc="st")
                P.op("vector", lambda v, j=j, pt=pt: v.tensor_copy(
                    dbt[:, j * 128:(j + 1) * 128], pt[:, 0:128]),
                    waits=[("st", P.c("st")), ("dS", dw)], inc="sv")
            P.op("sync", lambda s, g=g: s.dma_start(
                out=dst_dram[g * 128:(g + 1) * 128, :], in_=dbt[:]),
                waits=[("sv", P.c("sv"))], inc="dS", incd=16)
            P.op("sync", lambda s: s.nop(), waits=[("dS", P.c("dS"))])

    dump_fm.n = 0

    es2o = contextlib.ExitStack()
    O_all = mksb(es2o)("O_all", [128, NC, NT], BF16)
    es2 = contextlib.ExitStack()
    sb2 = mksb(es2)
    h1_fm = sb2("h1_fm", [128, NC, 2 * NT], BF16)

    es2a = contextlib.ExitStack()
    sb2a = mksb(es2a)
    xg_st = xs[:, 0:2, :].rearrange("p a b -> p (a b)")
    lnsc = xs[:, 2:4, :].rearrange("p a b -> p (a b)")
    h1tm = xs[:, 4, 0:1024].bitcast(BF16)

    dbgln = sb2a("dbgln", [128, 64], F32) if "dbg_LN" in dbg_out else None
    for half, src in ((0, "x_pre"), (1, "x_own")):
        for g in range(NG):
            buf = g % 2
            xt = xg_st[:, buf * D:(buf + 1) * D]
            xh = lnsc[:, buf * D:(buf + 1) * D]
            P.op("sync", lambda s, g=g, src=src, xt=xt: s.dma_start(
                out=xt, in_=ap[src][g * 128:(g + 1) * 128, :]),
                waits=[("ss", P.c("ss"))], inc="dS", incd=16)
            emit_ln_modulate(xt, xh, h1tm, CAt[:], SAt[:],
                             waits0=[("dS", P.c("dS"))])
            if dbgln is not None:
                gi = half * NG + g
                P.op("vector", lambda v, gi=gi: v.tensor_copy(
                    dbgln[:, gi * 4:gi * 4 + 4], stats[:, 3:7]), inc="sv")
            svw = P.c("sv")
            for j in range(NC):
                pt = psC if j % 2 == 0 else psD
                P.op("tensor", lambda t, j=j, pt=pt: t.transpose(
                    pt[:, 0:128], h1tm[:, j * 128:(j + 1) * 128], identb[:]),
                    waits=[("sv", svw), ("ss", P.c("ss"))], inc="st")
                tok0 = half * NT + g * 128
                P.op("scalar", lambda sc, j=j, pt=pt, tok0=tok0: sc.activation(
                    out=h1_fm[:, j, tok0:tok0 + 128], in_=pt[:, 0:128],
                    func=AF.Copy),
                    waits=[("st", P.c("st"))], inc="ss")
    if "dbg_h1" in dbg_out:
        dump_fm(dbg_out["dbg_h1"], h1_fm, 2 * NG, sb2a)
    if "dbg_H1F" in dbg_out:
        P.op("sync", lambda s: s.dma_start(
            out=dbg_out["dbg_H1F"][:, :],
            in_=h1_fm.rearrange("p a b -> p (a b)")),
            waits=[("ss", P.c("ss")), ("sv", P.c("sv"))], inc="dS", incd=16)
        P.op("sync", lambda s: s.nop(), waits=[("dS", P.c("dS"))])
    if dbgln is not None:
        P.op("sync", lambda s: s.dma_start(out=dbg_out["dbg_LN"][:, :],
                                           in_=dbgln[:]),
             waits=[("sv", P.c("sv"))], inc="dS", incd=16)
        P.op("sync", lambda s: s.nop(), waits=[("dS", P.c("dS"))])
    P.barrier()
    es2a.close()

    es2b = contextlib.ExitStack()
    sb2b = mksb(es2b)
    wn_st = sb2b("wn_st", [128, D], F32)
    wn_bf = sb2b("wn_bf", [128, D], BF16)
    K_h = xs[:, 0, :].bitcast(BF16)
    V_h = xs[:, 1, :].bitcast(BF16)
    arow = xs[:, 2, :].bitcast(BF16)
    Q_h = xs[:, 3, 0:512].bitcast(BF16)
    V_T = xs[:, 3, 512:1536].bitcast(BF16).rearrange("p (a b) -> p a b", b=128)
    den = xs[:, 3, 1536:1540]
    aT = xs[:, 4, :].bitcast(BF16).rearrange("p (a b) -> p a b", b=256)
    wqT = xs[:, 5, :].bitcast(BF16).rearrange("p (a b) -> p a b", b=128)
    wkT = xs[:, 6, :].bitcast(BF16).rearrange("p (a b) -> p a b", b=128)
    wvT = xs[:, 7, :].bitcast(BF16).rearrange("p (a b) -> p a b", b=128)

    def weight_prep(dram_ap, row0, dst, stage_st, stage_bf, scale_col=None,
                    sconst=1.0):
        P.op("sync", lambda s: s.dma_start(out=stage_st[:],
                                           in_=dram_ap[row0:row0 + 128, :]),
             waits=[("ss", P.c("ss"))], inc="dS", incd=16)
        if scale_col is not None:
            P.op("scalar", lambda sc: sc.activation(
                out=stage_bf[:], in_=stage_st[:], func=AF.Copy, scale=scale_col),
                waits=[("dS", P.c("dS")), ("st", P.c("st"))], inc="ss")
        else:
            P.op("scalar", lambda sc: sc.activation(
                out=stage_bf[:], in_=stage_st[:], func=AF.Copy, scale=sconst),
                waits=[("dS", P.c("dS")), ("st", P.c("st"))], inc="ss")
        for j in range(NC):
            pt = psC if j % 2 == 0 else psD
            P.op("tensor", lambda t, j=j, pt=pt: t.transpose(
                pt[:, 0:128], stage_bf[:, j * 128:(j + 1) * 128], identb[:]),
                waits=[("ss", P.c("ss"))], inc="st")
            P.op("scalar", lambda sc, j=j, pt=pt: sc.activation(
                out=dst[:, j, :], in_=pt[:, 0:128], func=AF.Copy),
                waits=[("st", P.c("st"))], inc="ss")

    for h in range(NH):
        weight_prep(ap["w_qkv"], h * 128, wqT, wn_st[:], wn_bf[:], sconst=S128)
        weight_prep(ap["w_qkv"], D + h * 128, wkT, wn_st[:], wn_bf[:])
        weight_prep(ap["w_qkv"], 2 * D + h * 128, wvT, wn_st[:], wn_bf[:])
        ppw = P.c("ss")
        for dst, wT, nt_, tk0, bias_col in [(K_h, wkT, 2 * NT, 0, NC + h),
                                            (Q_h, wqT, NT, NT, h),
                                            (V_h, wvT, 2 * NT, 0, 2 * NC + h)]:
            for g5 in range(nt_ // 512):
                bank = g5 % 4
                tok0 = tk0 + g5 * 512
                for j in range(NC):
                    P.op("tensor", lambda t, j=j, bank=bank, wT=wT, tok0=tok0:
                         t.matmul(psA[:, bank * 512:(bank + 1) * 512],
                                  wT[:, j, :], h1_fm[:, j, tok0:tok0 + 512],
                                  start=(j == 0), stop=(j == NC - 1)),
                         waits=[("ss", ppw), ("ss", P.c("ss")),
                                ("sv", P.c("sv"))],
                         inc=("st" if j == NC - 1 else None))
                P.op("scalar", lambda sc, g5=g5, dst=dst, bank=bank,
                     bias_col=bias_col: sc.activation(
                         out=dst[:, g5 * 512:(g5 + 1) * 512],
                         in_=psA[:, bank * 512:(bank + 1) * 512],
                         func=AF.Identity, bias=bqkv_sb[:, bias_col:bias_col + 1],
                         scale=1.0),
                     waits=[("st", P.c("st"))], inc="ss")
        if h == 0 and "dbg_K" in dbg_out:
            for nm, t_sb in (("dbg_K", K_h[:, 0:2 * NT]), ("dbg_Q", Q_h[:, 0:NT]),
                             ("dbg_V", V_h[:, 0:2 * NT])):
                P.op("sync", lambda s, nm=nm, t_sb=t_sb: s.dma_start(
                    out=dbg_out[nm][:, :], in_=t_sb),
                    waits=[("ss", P.c("ss")), ("sv", P.c("sv"))],
                    inc="dS", incd=16)
            P.op("sync", lambda s: s.nop(), waits=[("dS", P.c("dS"))])
        for kc in range(NC):
            pt = psC if kc % 2 == 0 else psD
            P.op("tensor", lambda t, kc=kc, pt=pt: t.transpose(
                pt[:, 0:128], V_h[:, kc * 128:(kc + 1) * 128], identb[:]),
                waits=[("ss", P.c("ss")), ("sv", P.c("sv"))], inc="st")
            P.op("scalar", lambda sc, kc=kc, pt=pt: sc.activation(
                out=V_T[:, kc, :], in_=pt[:, 0:128], func=AF.Copy),
                waits=[("st", P.c("st"))], inc="ss")
        if h == 0 and "dbg_AROW" in dbg_out:
            P.op("vector", lambda v: v.memset(arow[:, 0:4096], 0.0),
                 waits=[("ss", P.c("ss"))], inc="sv")
            P.op("vector", lambda v: v.memset(
                aT.rearrange("p a b -> p (a b)"), 0.0), inc="sv")
        kqvw = (P.c("ss"), P.c("sv"))
        for qp in range(NG // 2):
            for qq in range(2):
                qi = qp * 2 + qq
                kmax = NT + (qi + 1) * 128
                nk5 = (kmax + 511) // 512
                P.op("tensor", lambda t: t.nop(),
                     waits=[("ss", kqvw[0]), ("sv", kqvw[1]),
                            ("ss", P.c("ss")), ("sv", P.c("sv"))])
                for g5 in range(nk5):
                    w = min(512, kmax - g5 * 512)
                    P.op("tensor", lambda t, qi=qi, g5=g5, w=w: t.matmul(
                        psA[:, g5 * 512:g5 * 512 + w],
                        Q_h[:, qi * 128:(qi + 1) * 128],
                        K_h[:, g5 * 512:g5 * 512 + w], start=True, stop=True),
                        inc=("st" if g5 == nk5 - 1 else None))
                scw = P.c("st")
                P.op("vector", lambda v, kmax=kmax: v.tensor_tensor(
                    out=psA[:, kmax - 128:kmax], in0=psA[:, kmax - 128:kmax],
                    in1=causal[:], op=ALU.add), waits=[("st", scw)])
                P.op("vector", lambda v: v.tensor_scalar(
                    out=psA[:, 0:NT], in0=psA[:, 0:NT],
                    scalar1=kvm_sb[:, 0:1], scalar2=None,
                    op0=ALU.add, op1=ALU.bypass))
                P.op("vector", lambda v, kmax=kmax: v.reduce_max(
                    out=stats[:, 8:9], in_=psA[:, 0:kmax], axis=AXX, negate=True),
                    waits=[("ss", P.c("ss"))])
                P.op("vector", lambda v: v.drain(), inc="sv")
                P.op("scalar", lambda sc, qq=qq: sc.activation(
                    out=arow[:, qq * 2048:qq * 2048 + NT], in_=psA[:, 0:NT],
                    func=AF.Exp, bias=stats[:, 8:9], scale=1.0,
                    accum_out=den[:, 0:1]),
                    waits=[("sv", P.c("sv"))], inc="ss")
                P.op("scalar", lambda sc, qq=qq, kmax=kmax: sc.activation(
                    out=arow[:, qq * 2048 + NT:qq * 2048 + kmax],
                    in_=psA[:, NT:kmax], func=AF.Exp, bias=stats[:, 8:9],
                    scale=1.0, accum_out=den[:, 1:2]))
                P.op("scalar", lambda sc: sc.drain(), inc="ss")
                P.op("vector", lambda v: v.scalar_tensor_tensor(
                    out=den[:, 2:3], in0=den[:, 0:1], scalar=kvm_sb[:, 1:2],
                    in1=den[:, 1:2], op0=ALU.mult, op1=ALU.add),
                    waits=[("ss", P.c("ss"))])
                P.op("vector", lambda v: v.drain())
                P.op("vector", lambda v, qi=qi: v.reciprocal(
                    out=recipd[:, qi:qi + 1], in_=den[:, 2:3]))
                P.op("vector", lambda v: v.drain())
                P.op("vector", lambda v, qq=qq, qi=qi, kmax=kmax: v.tensor_scalar(
                    out=arow[:, qq * 2048:qq * 2048 + kmax],
                    in0=arow[:, qq * 2048:qq * 2048 + kmax],
                    scalar1=recipd[:, qi:qi + 1], scalar2=None,
                    op0=ALU.mult, op1=ALU.bypass), inc="sv")
            kmax2 = NT + (qp * 2 + 2) * 128
            nkc = kmax2 // 128
            sew = P.c("ss")
            for kc in range(nkc):
                for qq in range(2):
                    qi = qp * 2 + qq
                    kmaxq = NT + (qi + 1) * 128
                    if kc * 128 >= kmaxq:
                        P.op("vector", lambda v, kc=kc, qq=qq: v.memset(
                            aT[:, kc, qq * 128:(qq + 1) * 128], 0.0),
                            waits=[("st", P.c("st"))], inc="sv")
                        continue
                    pt = psC if (kc * 2 + qq) % 2 == 0 else psD
                    P.op("tensor", lambda t, kc=kc, qq=qq, pt=pt: t.transpose(
                        pt[:, 0:128],
                        arow[:, qq * 2048 + kc * 128: qq * 2048 + (kc + 1) * 128],
                        identb[:]),
                        waits=[("ss", sew), ("sv", P.c("sv"))], inc="st")
                    if kc < NG:
                        P.op("vector", lambda v, kc=kc, qq=qq, pt=pt:
                             v.tensor_scalar(
                                 out=aT[:, kc, qq * 128:(qq + 1) * 128],
                                 in0=pt[:, 0:128], scalar1=kvm_sb[:, 1:2],
                                 scalar2=None, op0=ALU.mult, op1=ALU.bypass),
                             waits=[("st", P.c("st"))], inc="sv")
                    else:
                        P.op("vector", lambda v, kc=kc, qq=qq, pt=pt: v.tensor_copy(
                            aT[:, kc, qq * 128:(qq + 1) * 128], pt[:, 0:128]),
                            waits=[("st", P.c("st"))], inc="sv")
            if h == 0 and qp == 0 and "dbg_AROW" in dbg_out:
                P.op("sync", lambda s: s.dma_start(
                    out=dbg_out["dbg_AROW"][:, :], in_=arow[:, 0:4096]),
                    waits=[("ss", P.c("ss")), ("sv", P.c("sv"))], inc="dS", incd=16)
                P.op("sync", lambda s: s.dma_start(
                    out=dbg_out["dbg_AT"][:, :],
                    in_=aT.rearrange("p a b -> p (a b)")),
                    waits=[("sv", P.c("sv"))], inc="dS", incd=16)
                P.op("sync", lambda s: s.dma_start(
                    out=dbg_out["dbg_DEN"][:, 0:2], in_=recipd[:, 0:2]),
                    waits=[("sv", P.c("sv"))], inc="dS", incd=16)
                P.op("sync", lambda s: s.nop(), waits=[("dS", P.c("dS"))])
            avw = P.c("sv")
            for kc in range(nkc):
                P.op("tensor", lambda t, kc=kc, nkc=nkc: t.matmul(
                    psB[:, 0:256], V_T[:, kc, :], aT[:, kc, :],
                    start=(kc == 0), stop=(kc == nkc - 1)),
                    waits=[("sv", avw)],
                    inc=("st" if kc == nkc - 1 else None))
            P.op("vector", lambda v, h=h, qp=qp: v.tensor_copy(
                O_all[:, h, qp * 256:(qp + 1) * 256], psB[:, 0:256]),
                waits=[("st", P.c("st"))], inc="sv")
    P.barrier()
    es2b.close()
    es2.close()

    es3 = contextlib.ExitStack()
    sb3 = mksb(es3)
    woutT = sb3("woutT", [128, NC, 128], BF16)
    tmpB = sb3("tmpB", [128, 512], F32)
    wn3_st = sb3("wn3_st", [128, D], F32)
    wn3_bf = sb3("wn3_bf", [128, D], BF16)

    for g in range(NG):
        P.op("sync", lambda s, g=g: s.dma_start(
            out=xs[:, g, :], in_=ap["x_own"][g * 128:(g + 1) * 128, :]),
            inc="dS", incd=16)
    xw = P.c("dS")
    for e in range(NC):
        weight_prep(ap["w_out"], e * 128, woutT, wn3_st[:], wn3_bf[:],
                    scale_col=modv[:, 2 * NC + e:2 * NC + e + 1])
        ppw = P.c("ss")
        for t2 in range(2):
            bank = t2 % 2
            for j in range(NC):
                P.op("tensor", lambda t, j=j, t2=t2, bank=bank: t.matmul(
                    psA[:, bank * 512:(bank + 1) * 512], woutT[:, j, :],
                    O_all[:, j, t2 * 512:(t2 + 1) * 512],
                    start=(j == 0), stop=(j == NC - 1)),
                    waits=[("ss", ppw), ("ss", P.c("ss")), ("sv", P.c("sv"))],
                    inc=("st" if j == NC - 1 else None))
            P.op("scalar", lambda sc, e=e, bank=bank: sc.activation(
                out=tmpB[:], in_=psA[:, bank * 512:(bank + 1) * 512],
                func=AF.Identity, bias=bout_ga[:, e:e + 1], scale=1.0),
                waits=[("st", P.c("st")), ("sv", P.c("sv"))], inc="ss")
            sw2 = P.c("ss")
            for gg in range(4):
                g = t2 * 4 + gg
                bank2 = 2 + gg % 2
                P.op("tensor", lambda t, gg=gg, bank2=bank2: t.transpose(
                    psA[:, bank2 * 512:bank2 * 512 + 128],
                    tmpB[:, gg * 128:(gg + 1) * 128], identf[:]),
                    waits=[("ss", sw2), ("sv", P.c("sv"))], inc="st")
                P.op("vector", lambda v, g=g, e=e, bank2=bank2:
                     v.scalar_tensor_tensor(
                         out=xs[:, g, e * 128:(e + 1) * 128],
                         in0=psA[:, bank2 * 512:bank2 * 512 + 128],
                         scalar=1.0,
                         in1=xs[:, g, e * 128:(e + 1) * 128],
                         op0=ALU.mult, op1=ALU.add),
                     waits=[("st", P.c("st")), ("dS", xw)], inc="sv")
    if "dbg_x1" in dbg_out:
        for g in range(NG):
            P.op("sync", lambda s, g=g: s.dma_start(
                out=dbg_out["dbg_x1"][g * 128:(g + 1) * 128, :], in_=xs[:, g, :]),
                waits=[("sv", P.c("sv"))], inc="dS", incd=16)
        P.op("sync", lambda s: s.nop(), waits=[("dS", P.c("dS"))])
    if "dbg_O" in dbg_out:
        dump_fm(dbg_out["dbg_O"], O_all, NG, sb3)
    P.barrier()
    es3.close()
    es2o.close()

    def emit_qact(src, out_bf, nblk, width, scr):
        nb2 = max(nblk, 64)
        a = qsc[:, 0:nblk]
        aw = qsc[:, 0:nb2]
        awi = aw.bitcast(I32)
        ef = qsc[:, 256:256 + nblk]
        efw = qsc[:, 256:256 + nb2]
        dq = qsc[:, 512:512 + nblk]
        dqw = qsc[:, 512:512 + nb2]
        s3 = src.rearrange("p (b k) -> p b k", k=32)
        z3 = scr.rearrange("p (b k) -> p b k", k=32)
        o3 = out_bf.rearrange("p (b k) -> p b k", k=32)
        P.op("vector", lambda v: v.tensor_reduce(
            out=a, in_=s3, axis=AXX, op=ALU.max, apply_absolute_value=True),
            waits=[("ss", P.c("ss"))])
        P.op("vector", lambda v: v.drain())
        P.op("vector", lambda v: v.tensor_scalar(
            out=aw, in0=aw, scalar1=1e-30, scalar2=None, op0=ALU.max, op1=ALU.bypass))
        P.op("vector", lambda v: v.tensor_scalar(
            out=awi, in0=awi, scalar1=0x7F800000, scalar2=None,
            op0=ALU.bitwise_and, op1=ALU.bypass))
        P.op("vector", lambda v: v.tensor_copy(efw, awi))
        P.op("vector", lambda v: v.tensor_scalar(
            out=aw, in0=efw, scalar1=-1.0, scalar2=float(259 * B23),
            op0=ALU.mult, op1=ALU.add))
        P.op("vector", lambda v: v.tensor_copy(awi, aw))
        P.op("vector", lambda v: v.tensor_scalar(
            out=dqw, in0=efw, scalar1=float(4 * B23), scalar2=None,
            op0=ALU.add, op1=ALU.bypass))
        P.op("vector", lambda v: v.tensor_copy(dqw.bitcast(I32), dqw))
        P.op("vector", lambda v: v.drain())
        P.op("vector", lambda v: v.tensor_tensor(
            out=z3, in0=s3,
            in1=a.unsqueeze(2).broadcast_to((128, nblk, 32)), op=ALU.mult))
        P.op("vector", lambda v: v.tensor_scalar(
            out=scr, in0=scr, scalar1=60.0, scalar2=-60.0, op0=ALU.min,
            op1=ALU.max), inc="sv")
        P.op("scalar", lambda sc: sc.activation(
            out=qf8[:, 0:width], in_=scr, func=AF.Copy, scale=1.0 / 512.0),
            waits=[("sv", P.c("sv"))], inc="ss")
        f83 = qf8[:, 0:width].rearrange("p (b k) -> p b k", k=32)
        P.op("vector", lambda v: v.tensor_tensor(
            out=o3, in0=f83,
            in1=dq.unsqueeze(2).broadcast_to((128, nblk, 32)), op=ALU.mult),
            waits=[("ss", P.c("ss")), ("st", P.c("st"))], inc="sv")

    def emit_qw(src, out_bf, nblk, width, scr, scr2, scr3, row_scale=None,
                wait0=()):
        nb2 = max(nblk, 64)
        a = qsc[:, 0:nblk]
        aw = qsc[:, 0:nb2]
        awi = aw.bitcast(I32)
        ef = qsc[:, 256:256 + nblk]
        efw = qsc[:, 256:256 + nb2]
        dq = qsc[:, 512:512 + nblk]
        dqw = qsc[:, 512:512 + nb2]
        s3 = src.rearrange("p (b k) -> p b k", k=32)
        v3 = scr.rearrange("p (b k) -> p b k", k=32)
        o3 = out_bf.rearrange("p (b k) -> p b k", k=32)
        P.op("vector", lambda v: v.tensor_reduce(
            out=a, in_=s3, axis=AXX, op=ALU.max, apply_absolute_value=True),
            waits=[("ss", P.c("ss"))] + list(wait0))
        P.op("vector", lambda v: v.drain())
        P.op("vector", lambda v: v.tensor_scalar(
            out=aw, in0=aw, scalar1=1e-30, scalar2=None, op0=ALU.max, op1=ALU.bypass))
        P.op("vector", lambda v: v.tensor_scalar(
            out=awi, in0=awi, scalar1=0x7F800000, scalar2=None,
            op0=ALU.bitwise_and, op1=ALU.bypass))
        P.op("vector", lambda v: v.tensor_copy(efw, awi))
        P.op("vector", lambda v: v.tensor_scalar(
            out=aw, in0=efw, scalar1=-1.0, scalar2=float(257 * B23),
            op0=ALU.mult, op1=ALU.add))
        P.op("vector", lambda v: v.tensor_copy(awi, aw))
        P.op("vector", lambda v: v.tensor_scalar(
            out=dqw, in0=efw, scalar1=float(6 * B23), scalar2=None,
            op0=ALU.add, op1=ALU.bypass))
        P.op("vector", lambda v: v.tensor_copy(dqw.bitcast(I32), dqw))
        P.op("vector", lambda v: v.drain())
        if row_scale is not None:
            P.op("vector", lambda v: v.tensor_scalar(
                out=dqw, in0=dqw, scalar1=row_scale, scalar2=None,
                op0=ALU.mult, op1=ALU.bypass))
            P.op("vector", lambda v: v.drain())
        P.op("vector", lambda v: v.tensor_tensor(
            out=v3, in0=s3,
            in1=a.unsqueeze(2).broadcast_to((128, nblk, 32)), op=ALU.mult))
        P.op("vector", lambda v: v.tensor_scalar(
            out=scr, in0=scr, scalar1=12.0, scalar2=-12.0, op0=ALU.min,
            op1=ALU.max), inc="sv")
        P.op("scalar", lambda sc: sc.activation(out=scr2, in_=scr, func=AF.Square),
             waits=[("sv", P.c("sv"))], inc="ss")
        P.op("vector", lambda v: v.tensor_scalar(
            out=scr3, in0=scr2, scalar1=16.0, scalar2=-0.5,
            op0=ALU.is_ge, op1=ALU.mult), waits=[("ss", P.c("ss"))])
        P.op("vector", lambda v: v.tensor_scalar(
            out=scr2, in0=scr2, scalar1=64.0, scalar2=-0.25,
            op0=ALU.is_ge, op1=ALU.mult))
        P.op("vector", lambda v: v.scalar_tensor_tensor(
            out=scr3, in0=scr3, scalar=1.0, in1=scr2, op0=ALU.add, op1=ALU.add))
        P.op("vector", lambda v: v.tensor_tensor(
            out=scr, in0=scr, in1=scr3, op=ALU.mult))
        P.op("vector", lambda v: v.tensor_scalar(
            out=scr2, in0=scr2, scalar1=-6.0, scalar2=None, op0=ALU.mult,
            op1=ALU.bypass))
        P.op("vector", lambda v: v.tensor_scalar(
            out=scr3, in0=scr3, scalar1=-2.0, scalar2=3.0, op0=ALU.mult,
            op1=ALU.add))
        P.op("vector", lambda v: v.tensor_tensor(
            out=scr3, in0=scr3, in1=scr2, op=ALU.add), inc="sv")
        P.op("scalar", lambda sc: sc.activation(
            out=qf8[:, 0:width], in_=scr, func=AF.Copy, scale=1.0 / 512.0),
            waits=[("sv", P.c("sv"))], inc="ss")
        P.op("vector", lambda v: v.tensor_tensor(
            out=scr, in0=qf8[:, 0:width], in1=scr3, op=ALU.mult),
            waits=[("ss", P.c("ss"))])
        P.op("vector", lambda v: v.tensor_tensor(
            out=o3, in0=scr.rearrange("p (b k) -> p b k", k=32),
            in1=dq.unsqueeze(2).broadcast_to((128, nblk, 32)), op=ALU.mult),
            waits=[("st", P.c("st"))], inc="sv")

    qh2_fm = sb("qh2_fm", [128, NC, NT], BF16)
    qsc = sb("qsc", [128, 3 * 256], F32)
    P.op("vector", lambda v: v.memset(qsc[:], 1.0), inc="sv")

    es4a = contextlib.ExitStack()
    sb4a = mksb(es4a)
    h2sc = sb4a("h2sc", [128, D], F32)
    h2bf = sb4a("h2bf", [128, D], BF16)
    h2z = sb4a("h2z", [128, D], F32)

    bcast_row(CAt[:], mod_sbcb[0:1, D:2 * D], True)
    bcast_row(SAt[:], mod_sbcb[0:1, 0:D], False)
    for g in range(NG):
        emit_ln_modulate(xs[:, g, :], h2sc[:], h2sc[:], CAt[:], SAt[:])
        emit_qact(h2sc[:], h2bf[:], D // 32, D, h2z[:])
        svw = P.c("sv")
        for j in range(NC):
            pt = psC if j % 2 == 0 else psD
            P.op("tensor", lambda t, j=j, pt=pt: t.transpose(
                pt[:, 0:128], h2bf[:, j * 128:(j + 1) * 128], identb[:]),
                waits=[("sv", svw), ("ss", P.c("ss"))], inc="st")
            P.op("scalar", lambda sc, j=j, g=g, pt=pt: sc.activation(
                out=qh2_fm[:, j, g * 128:(g + 1) * 128], in_=pt[:, 0:128],
                func=AF.Copy),
                waits=[("st", P.c("st"))], inc="ss")
    if "dbg_qh2" in dbg_out:
        dump_fm(dbg_out["dbg_qh2"], qh2_fm, NG, sb4a)
    P.barrier()
    es4a.close()

    es4 = contextlib.ExitStack()
    sb4 = mksb(es4)
    wdn = sb4("wdn", [128, 2 * 1024], F32)
    wqbf = sb4("wqbf", [128, 1024], BF16)
    wq_s3 = sb4("wq_s3", [128, 1024], F32)
    qgT = sb4("qgT", [128, NC, 2 * IB], BF16)
    quT = sb4("quT", [128, NC, 2 * IB], BF16)
    interf = sb4("interf", [128, NG * IB], F32)
    sil = sb4("sil", [128, IB], F32)
    qi_tm = sb4("qi_tm", [128, NG * IB], BF16)
    qi_scr = sb4("qi_scr", [128, NG * IB], F32)
    NBW = 1024 // 32

    def wq_tile(src, row_scale=None):
        emit_qw(src, wqbf[:], NBW, 1024, qi_scr[:, 0:1024],
                interf[:, 0:1024], wq_s3[:], row_scale=row_scale,
                wait0=[("dS", P.c("dS"))])

    for dg_ in range(NC):
        for qtr in range(8):
            buf = (dg_ * 8 + qtr) % 2
            st_ = wdn[:, buf * 1024:(buf + 1) * 1024]
            P.op("sync", lambda s, dg_=dg_, qtr=qtr, st_=st_: s.dma_start(
                out=st_, in_=ap["w_down"][dg_ * 128:(dg_ + 1) * 128,
                                          qtr * 1024:(qtr + 1) * 1024]),
                waits=[("sv", P.c("sv"))], inc="dS", incd=16)
            P.op("sync", lambda s: s.nop(), waits=[("dS", P.c("dS"))])
            wq_tile(st_, row_scale=modv[:, 5 * NC + dg_:5 * NC + dg_ + 1])
            svw = P.c("sv")
            for t_ in range(8):
                ic = qtr * 8 + t_
                pt = psC if t_ % 2 == 0 else psD
                P.op("tensor", lambda t, t_=t_, pt=pt: t.transpose(
                    pt[:, 0:128], wqbf[:, t_ * 128:(t_ + 1) * 128], identb[:]),
                    waits=[("sv", svw), ("ss", P.c("ss"))], inc="st")
                psum_to_dram_bf16(pt, qwdT_d[ic, :, dg_ * 128:(dg_ + 1) * 128])

    for ib in range(NIB):
        half = ib % 2
        for mi, (wdram, qT) in enumerate([(ap["w_gate"], qgT), (ap["w_up"], quT)]):
            for r in range(IB // 128):
                for hc in range(2):
                    buf = (ib * 8 + mi * 4 + r * 2 + hc) % 2
                    st_ = wdn[:, buf * 1024:(buf + 1) * 1024]
                    P.op("sync", lambda s, st_=st_, wdram=wdram, ib=ib, r=r,
                         hc=hc: s.dma_start(
                             out=st_,
                             in_=wdram[ib * IB + r * 128: ib * IB + (r + 1) * 128,
                                       hc * 1024:(hc + 1) * 1024]),
                        waits=[("sv", P.c("sv"))], inc="dS", incd=16)
                    P.op("sync", lambda s: s.nop(), waits=[("dS", P.c("dS"))])
                    wq_tile(st_)
                    svw = P.c("sv")
                    for t_ in range(8):
                        j = hc * 8 + t_
                        pt = psC if t_ % 2 == 0 else psD
                        P.op("tensor", lambda t, t_=t_, pt=pt: t.transpose(
                            pt[:, 0:128], wqbf[:, t_ * 128:(t_ + 1) * 128],
                            identb[:]),
                            waits=[("sv", svw), ("ss", P.c("ss"))], inc="st")
                        P.op("scalar", lambda sc, j=j, pt=pt, qT=qT, half=half,
                             r=r: sc.activation(
                                 out=qT[:, j, half * IB + r * 128:
                                        half * IB + (r + 1) * 128],
                                 in_=pt[:, 0:128], func=AF.Copy),
                             waits=[("st", P.c("st"))], inc="ss")
        wpw = P.c("ss")
        for g in range(NG):
            bkg, bku = 2 * (g % 2), 2 * (g % 2) + 1
            for bk, qT in ((bkg, qgT), (bku, quT)):
                for j in range(NC):
                    P.op("tensor", lambda t, j=j, g=g, half=half, bk=bk, qT=qT:
                         t.matmul(psA[:, bk * 512:bk * 512 + IB],
                                  qh2_fm[:, j, g * 128:(g + 1) * 128],
                                  qT[:, j, half * IB:(half + 1) * IB],
                                  start=(j == 0), stop=(j == NC - 1)),
                         waits=[("ss", wpw), ("ss", P.c("ss")), ("sv", P.c("sv"))],
                         inc=("st" if j == NC - 1 else None))
            stw = P.c("st")
            P.op("scalar", lambda sc, bkg=bkg: sc.activation(
                out=sil[:], in_=psA[:, bkg * 512:bkg * 512 + IB], func=AF.Silu),
                waits=[("st", stw)], inc="ss")
            P.op("vector", lambda v, g=g, bku=bku: v.tensor_tensor(
                out=interf[:, g * IB:(g + 1) * IB], in0=sil[:],
                in1=psA[:, bku * 512:bku * 512 + IB], op=ALU.mult),
                waits=[("ss", P.c("ss"))], inc="sv")
        emit_qact(interf[:], qi_tm[:], NG * IB // 32, NG * IB, qi_scr[:])
        svw = P.c("sv")
        for g in range(NG):
            for t_ in range(IB // 128):
                ic = ib * 2 + t_
                pt = psC if (g * 2 + t_) % 2 == 0 else psD
                P.op("tensor", lambda tt, g=g, t_=t_, pt=pt: tt.transpose(
                    pt[:, 0:128],
                    qi_tm[:, g * IB + t_ * 128: g * IB + (t_ + 1) * 128],
                    identb[:]),
                    waits=[("sv", svw), ("ss", P.c("ss"))], inc="st")
                psum_to_dram_bf16(pt, qiT_d[ic, :, g * 128:(g + 1) * 128])
    P.barrier()
    es4.close()

    es5 = contextlib.ExitStack()
    sb5 = mksb(es5)
    qi_sb = sb5("qi_sb", [128, 64 * 512], BF16)
    wd_sb = sb5("wd_sb", [128, 3 * 512], BF16)

    dg_all = P.c("dG")
    pairs = [(t2, dq_) for t2 in range(2) for dq_ in range(4)]

    def wd_slot2_load(pi, waits):
        _, dq2 = pairs[pi]
        P.op("sync", lambda s, dq2=dq2: s.dma_start(
            out=wd_sb[:, 2 * 512:3 * 512],
            in_=qwdT_d[0, :, dq2 * 512:(dq2 + 1) * 512]),
            waits=waits, inc="dS", incd=16)

    wd_slot2_load(0, [("st", P.c("st")), ("dG", dg_all)])
    for pi, (t2, dq_) in enumerate(pairs):
        if dq_ == 0:
            for ic in range(64):
                P.op("sync", lambda s, ic=ic, t2=t2: s.dma_start(
                    out=qi_sb[:, ic * 512:(ic + 1) * 512],
                    in_=qiT_d[ic, :, t2 * 512:(t2 + 1) * 512]),
                    waits=[("dG", dg_all), ("sv", P.c("sv"))],
                    inc="dS", incd=16)
        qiw = P.c("dS")
        for ic in range(64):
            slot = 2 if ic == 0 else (ic - 1) % 2
            stw_prev = P.c("st")
            mmw = P.c("dS")
            for gg in range(4):
                P.op("tensor", lambda t, ic=ic, gg=gg, slot=slot:
                     t.matmul(
                         psA[:, gg * 512:(gg + 1) * 512],
                         qi_sb[:, ic * 512 + gg * 128: ic * 512 + (gg + 1) * 128],
                         wd_sb[:, slot * 512:(slot + 1) * 512],
                         start=(ic == 0), stop=(ic == 63)),
                     waits=[("dS", max(qiw, mmw)), ("sv", P.c("sv")),
                            ("ss", P.c("ss"))],
                     inc=("st" if gg == 3 else None))
            if ic < 63:
                P.op("sync", lambda s, ic=ic, dq_=dq_: s.dma_start(
                    out=wd_sb[:, (ic % 2) * 512:(ic % 2 + 1) * 512],
                    in_=qwdT_d[ic + 1, :, dq_ * 512:(dq_ + 1) * 512]),
                    waits=[("st", stw_prev)], inc="dS", incd=16)
            if ic == 1 and pi + 1 < len(pairs):
                wd_slot2_load(pi + 1, [("st", P.c("st"))])
        for gg in range(4):
            g = t2 * 4 + gg
            P.op("vector", lambda v, g=g, gg=gg, dq_=dq_:
                 v.scalar_tensor_tensor(
                     out=xs[:, g, dq_ * 512:(dq_ + 1) * 512],
                     in0=psA[:, gg * 512:(gg + 1) * 512], scalar=1.0,
                     in1=xs[:, g, dq_ * 512:(dq_ + 1) * 512],
                     op0=ALU.mult, op1=ALU.add),
                 waits=[("st", P.c("st"))], inc="sv")

    P.op("vector", lambda v: v.drain(), inc="sv")
    for g in range(NG):
        P.op("gpsimd", lambda gp, g=g: gp.dma_start(
            out=y[g * 128:(g + 1) * 128, :], in_=xs[:, g, :]),
            waits=[("sv", P.c("sv"))], inc="dG", incd=16)
    P.op("gpsimd", lambda gp: gp.nop(), waits=[("dG", P.c("dG"))])
    es5.close()

    with nc.Block() as block:
        P.emit(block, sems)
    es.close()
    return nc


def kernel(L0_attn_x, L0_attn_w_qkv, L0_attn_b_qkv, L0_attn_w_out, L0_attn_b_out,
           L0_ffn_w_gate, L0_ffn_w_up, L0_ffn_w_down, L0_c, L0_wmod, _dbg=(),
           _cores=None):
    nc = build_nc(dbg=_dbg)
    x = np.ascontiguousarray(np.asarray(L0_attn_x, dtype=np.float32))
    cores = list(range(8)) if _cores is None else _cores
    in_maps = []
    for c in cores:
        b, half = c // 2, c % 2
        in_maps.append({
            "x_own": np.ascontiguousarray(x[b, half * NT:(half + 1) * NT]),
            "x_pre": np.ascontiguousarray(x[b, 0:NT]),
            "w_qkv": np.asarray(L0_attn_w_qkv, dtype=np.float32),
            "b_qkv": np.asarray(L0_attn_b_qkv, dtype=np.float32),
            "w_out": np.asarray(L0_attn_w_out, dtype=np.float32),
            "b_out": np.asarray(L0_attn_b_out, dtype=np.float32),
            "w_gate": np.asarray(L0_ffn_w_gate, dtype=np.float32),
            "w_up": np.asarray(L0_ffn_w_up, dtype=np.float32),
            "w_down": np.asarray(L0_ffn_w_down, dtype=np.float32),
            "c_row": np.ascontiguousarray(np.asarray(L0_c, dtype=np.float32)[b:b + 1, :]),
            "wmod": np.asarray(L0_wmod, dtype=np.float32),
            "kvm": np.array([[0.0, 1.0]] if half == 1 else [[NEG, 0.0]],
                            dtype=np.float32),
        })
    res = run_bass_kernel_spmd(nc, in_maps, core_ids=cores)
    out = np.zeros((4, 2048, D), dtype=np.float32)
    for i, c in enumerate(cores):
        b, half = c // 2, c % 2
        out[b, half * NT:(half + 1) * NT] = res.results[i]["y"]
    if _dbg:
        return out, res
    return out
